# version 20
# baseline (speedup 1.0000x reference)
import numpy as np
import ml_dtypes
from contextlib import ExitStack

import concourse.bass as bass
import concourse.mybir as mybir
from concourse.bass_utils import run_bass_kernel_spmd

F32 = mybir.dt.float32
BF16 = mybir.dt.bfloat16
I32 = mybir.dt.int32
AF = mybir.ActivationFunctionType
ALU = mybir.AluOpType
AX = mybir.AxisListType

D = 1024
S = 2048
HID = 3072
NH = 16
KT = 8
HT = 24
NG = 4
GH = 4
CDOWN = 0.5 / (127.0 * 127.0)
CSILU = 1.0 / 127.0

ENGS = ("sync", "gpsimd", "vector", "scalar", "tensor")

A_XB = 0
A_SQ = 8192
A_TB = 16384
A_HT = 0
A_QK = (0, 8192)
A_P = 16384
A_VX = (20480, 24640)
A_SQN = 28800
R_GU = 0
R_DN = 4096


def build_nc(stop_phase=99, dbg=None, ffn_kw=None):
    nc = bass.Bass("TRN2", target_bir_lowering=False,
                   detect_race_conditions=False)

    XT = nc.dram_tensor("xt", [KT, 128, S], F32, kind="ExternalInput").ap()
    WQK = nc.dram_tensor("wqk", [16, 128, KT, 128], F32, kind="ExternalInput").ap()
    WV = nc.dram_tensor("wv", [NG, 128, KT, 256], F32, kind="ExternalInput").ap()
    WFC = nc.dram_tensor("wfc", [8, 128, KT, 128], F32, kind="ExternalInput").ap()
    WG = nc.dram_tensor("wg", [HT, 128, KT, 128], I32, kind="ExternalInput").ap()
    WU = nc.dram_tensor("wu", [HT, 128, KT, 128], I32, kind="ExternalInput").ap()
    WD = nc.dram_tensor("wd", [8, 128, HT, 128], I32, kind="ExternalInput").ap()
    BQK = nc.dram_tensor("bqk", [128, 16], F32, kind="ExternalInput").ap()
    BV = nc.dram_tensor("bv", [1, D], F32, kind="ExternalInput").ap()
    BFC = nc.dram_tensor("bfc", [128, 8], F32, kind="ExternalInput").ap()
    OUT = nc.dram_tensor("out", [KT, 128, S], F32, kind="ExternalOutput").ap()
    OT = nc.dram_tensor("ot", [KT, 128, S], BF16, kind="Internal").ap()

    es = ExitStack()
    sbt = lambda name, shape, dt: es.enter_context(nc.sbuf_tensor(name, shape, dt))

    rT = sbt("rT", [128, KT * S], F32)
    yT = sbt("yT", [128, KT * S], BF16)
    ARN = sbt("arn", [128, 32896], BF16)
    RING = sbt("ring", [128, 10240], BF16)
    obuf = sbt("obuf", [128, 2 * 1024], BF16)
    sgb = sbt("sgb", [128, 2 * 512], F32)
    tfb = sbt("tfb", [128, 2 * 512], F32)
    fcin = obuf
    rvb = sbt("rvb", [128, 1024], BF16)
    ones = sbt("ones", [128, 512], BF16)
    rowA = sgb
    rowB = sbt("rowB", [1, 1024], F32)
    rowC = tfb
    rB16 = sbt("rB16", [1, 1024], BF16)
    rE16 = sbt("rE16", [1, 1024], BF16)
    bqk = sbt("bqk_sb", [128, 16], F32)
    bfc = sbt("bfc_sb", [128, 8], F32)
    bv16 = sbt("bv16", [1, D], BF16)
    negc = sbt("negc", [128, NG], F32)
    nmx = sbt("nmx", [1, 16], F32)
    nb16 = sbt("nb16", [1, 16], BF16)
    epsr = sbt("epsr", [1, 1], F32)
    PS = es.enter_context(nc.psum_tensor("ps", [128, 4096], F32))

    SEMNAMES = ("d0", "gv", "u0", "u1", "u2", "u3", "n0", "n1",
                "fcw", "f0", "f1", "f2", "f3", "o0", "o1", "do", "t", "v", "s")
    sems = {k: es.enter_context(nc.semaphore("sem_" + k)) for k in SEMNAMES}
    OPS = {e: [] for e in ENGS}
    CTR = {k: 0 for k in SEMNAMES}
    WAITED = {e: {k: 0 for k in SEMNAMES} for e in ENGS}

    def w(eng, sem, val):
        if not val or val <= WAITED[eng][sem]:
            return
        WAITED[eng][sem] = val
        OPS[eng].append(lambda e, _s=sems[sem], _v=val: e.wait_ge(_s, _v))

    def go(eng, fn, sem=None, inc=1):
        if sem is None:
            OPS[eng].append(fn)
            return None
        CTR[sem] += inc
        OPS[eng].append(lambda e, _f=fn, _s=sems[sem], _i=inc: _f(e).then_inc(_s, _i))
        return CTR[sem]

    def mark(eng, sem):
        CTR[sem] += 1
        prev = OPS[eng].pop()
        OPS[eng].append(lambda e, _p=prev, _s=sems[sem]: _p(e).then_inc(_s, 1))
        return CTR[sem]

    def dma(eng, sem, out_ap, in_ap):
        return go(eng, lambda e, _o=out_ap, _i=in_ap: e.dma_start(_o, _i),
                  sem=sem, inc=16)

    def wy(eng, v_yT):
        w(eng, "v", v_yT["v"])
        if v_yT.get("gv"):
            w(eng, "gv", v_yT["gv"])

    XSEM = ("d0", "f1", "f2", "f3")
    d_xk = {}
    for c in range(2):
        for k in range(KT):
            p = c * KT + k
            d_xk[(c, k)] = (XSEM[p % 4], dma(
                "sync", XSEM[p % 4],
                rT[:, k * S + c * 1024: k * S + c * 1024 + 1024],
                XT[k][:, c * 1024: c * 1024 + 1024]))
    d_bqk = dma("sync", "f0", bqk[:], BQK)
    d_bfc = dma("sync", "f0", bfc[:], BFC)
    d_x = d_xk
    d_bv = dma("gpsimd", "gv", bv16[:], BV)
    v_ones = go("vector", lambda e: e.memset(ones[:], 1.0), sem="v")
    v_eps = go("vector", lambda e: e.memset(epsr[:], 1e-5), sem="v")

    def snap():
        return dict(CTR)

    def emit_ln(dep_v=None, dep_d=None, guard=None):
        v_done = None
        g_done = None
        gd = (guard or {}).get("v", 0)
        prev_row_v = (guard or {}).get("v")
        prev_cp_v = (guard or {}).get("v")
        if guard:
            w("tensor", "s", guard.get("s"))
        prev_s = None
        for c in range(2):
            cc = c * 1024
            v_cp = {}
            cp_eng = {}
            s_sq = {}
            for k in range(KT):
                eng = "vector"
                cp_eng[k] = eng
                if dep_d: w(eng, dep_d[(c, k)][0], dep_d[(c, k)][1])
                w(eng, "v", v_ones)
                if dep_v: w(eng, "v", dep_v)
                if gd: w(eng, "v", gd)
                if prev_s: w(eng, "s", prev_s)
                v_cp[k] = go(eng, lambda e, _k=k, _cc=cc: e.tensor_copy(
                    ARN[:, A_XB + _k * 1024: A_XB + _k * 1024 + 1024],
                    rT[:, _k * S + _cc: _k * S + _cc + 1024]),
                    sem=("v" if eng == "vector" else "gv"))
            for k in range(KT):
                w("scalar", "v" if cp_eng[k] == "vector" else "gv", v_cp[k])
                s_sq[k] = go("scalar", lambda e, _k=k: e.activation(
                    ARN[:, A_SQ + _k * 1024: A_SQ + _k * 1024 + 1024],
                    ARN[:, A_XB + _k * 1024: A_XB + _k * 1024 + 1024],
                    AF.Square), sem="s")
            if prev_row_v: w("tensor", "v", prev_row_v)
            for k in range(KT):
                w("tensor", "v" if cp_eng[k] == "vector" else "gv", v_cp[k])
                for n in range(2):
                    go("tensor", lambda e, _k=k, _n=n: e.matmul(
                        PS[0:1, _n * 512:_n * 512 + 512], ones[:, 0:1],
                        ARN[:, A_XB + _k * 1024 + _n * 512: A_XB + _k * 1024 + _n * 512 + 512],
                        start=(_k == 0), stop=(_k == KT - 1)))
            t_sums = mark("tensor", "t")
            for k in range(KT):
                w("tensor", "s", s_sq[k])
                for n in range(2):
                    go("tensor", lambda e, _k=k, _n=n: e.matmul(
                        PS[0:1, 1024 + _n * 512: 1024 + _n * 512 + 512], ones[:, 0:1],
                        ARN[:, A_SQ + _k * 1024 + _n * 512: A_SQ + _k * 1024 + _n * 512 + 512],
                        start=(_k == 0), stop=(_k == KT - 1)))
            t_sq = mark("tensor", "t")
            v_row = {}
            s_row = {}
            for n in range(2):
                sl = slice(n * 512, n * 512 + 512)
                w("vector", "t", t_sums)
                w("vector", "t", t_sq)
                go("vector", lambda e, _sl=sl: e.tensor_scalar(
                    rowA[0:1, _sl], PS[0:1, _sl], 1.0 / D, None, ALU.mult))
                go("vector", lambda e, _sl=sl, _n=n: e.tensor_scalar(
                    rowB[0:1, _sl], PS[0:1, 1024 + _n * 512:1024 + _n * 512 + 512],
                    1.0 / D, None, ALU.mult))
                go("vector", lambda e, _sl=sl: e.tensor_mul(
                    rowC[0:1, _sl], rowA[0:1, _sl], rowA[0:1, _sl]))
                v_row[n] = go("vector", lambda e, _sl=sl: e.tensor_sub(
                    rowB[0:1, _sl], rowB[0:1, _sl], rowC[0:1, _sl]), sem="v")
            for n in range(2):
                sl = slice(n * 512, n * 512 + 512)
                w("scalar", "v", v_row[n])
                w("scalar", "v", v_eps)
                go("scalar", lambda e, _sl=sl: e.activation(
                    rowB[0:1, _sl], rowB[0:1, _sl], AF.Ln, bias=epsr[0:1, 0:1]))
                s_row[n] = go("scalar", lambda e, _sl=sl: e.activation(
                    rB16[0:1, _sl], rowB[0:1, _sl], AF.Exp, scale=-0.5), sem="s")
            prev_s = s_row[1]
            v_e = {}
            for n in range(2):
                sl = slice(n * 512, n * 512 + 512)
                w("vector", "s", s_row[n])
                v_e[n] = go("vector", lambda e, _sl=sl: e.tensor_mul(
                    rE16[0:1, _sl], rowA[0:1, _sl], rB16[0:1, _sl]), sem="v")
            prev_row_v = v_e[1]
            if prev_cp_v: w("tensor", "v", prev_cp_v)
            for n in range(2):
                w("tensor", "s", s_row[n])
                w("tensor", "v", v_e[n])
                go("tensor", lambda e, _n=n: e.matmul(
                    PS[:, 2048 + _n * 512: 2048 + _n * 512 + 512],
                    ones[0:1, 0:128], rB16[0:1, _n * 512:_n * 512 + 512],
                    start=True, stop=True))
                go("tensor", lambda e, _n=n: e.matmul(
                    PS[:, 3072 + _n * 512: 3072 + _n * 512 + 512],
                    ones[0:1, 0:128], rE16[0:1, _n * 512:_n * 512 + 512],
                    start=True, stop=True))
            t_bc = mark("tensor", "t")
            w("vector", "t", t_bc)
            if g_done:
                w("vector", "gv", g_done)
            go("vector", lambda e: e.tensor_copy(ARN[:, A_TB:A_TB + 1024],
                                                 PS[:, 2048:3072]))
            prev_cp_v = go("vector", lambda e: e.tensor_copy(
                ARN[:, A_TB + 1024:A_TB + 2048], PS[:, 3072:4096]), sem="v")
            for k in range(KT):
                eng = cp_eng[k]
                if eng == "gpsimd":
                    w(eng, "v", prev_cp_v)
                go(eng, lambda e, _k=k: e.tensor_mul(
                    ARN[:, A_XB + _k * 1024: A_XB + _k * 1024 + 1024],
                    ARN[:, A_XB + _k * 1024: A_XB + _k * 1024 + 1024],
                    ARN[:, A_TB:A_TB + 1024]))
                d_ap = go(eng, lambda e, _k=k, _cc=cc: e.tensor_sub(
                    yT[:, _k * S + _cc: _k * S + _cc + 1024],
                    ARN[:, A_XB + _k * 1024: A_XB + _k * 1024 + 1024],
                    ARN[:, A_TB + 1024:A_TB + 2048]),
                    sem=("v" if eng == "vector" else "gv"))
                if eng == "vector":
                    v_done = d_ap
                else:
                    g_done = d_ap
        return {"v": v_done, "gv": g_done}

    ffn_state = {"gu_slot_t": [0] * 4, "dn_slot_t": [0, 0],
                 "gate_bank_s": [0, 0], "up_bank_v": [0, 0],
                 "dn_bank_v": [0, 0, 0], "sg_slot_v": [0, 0],
                 "dt_done_v": {}}

    def emit_ffn(v_yT, guard=None, nsc=2, nht=HT, do_down=True):
        st = ffn_state
        if guard:
            st["gate_bank_s"] = [max(x, guard.get("s", 0)) for x in st["gate_bank_s"]]
            st["up_bank_v"] = [max(x, guard.get("v", 0)) for x in st["up_bank_v"]]
            st["dn_bank_v"] = [max(x, guard.get("v", 0)) for x in st["dn_bank_v"]]
        v_last = None
        gu_seq = 0
        dn_seq = 0
        for sc in range(nsc):
            cc = sc * 1024
            hbase = A_HT
            v_h = {}
            for ht in range(nht):
                for wt, W_ in ((0, WG), (1, WU)):
                    slot = gu_seq % 4
                    w("gpsimd", "t", st["gu_slot_t"][slot])
                    g_w = dma("gpsimd", "u%d" % slot,
                              RING[:, R_GU + slot * 1024: R_GU + slot * 1024 + 1024],
                              W_[ht])
                    if wt == 0:
                        g_gate, gate_slot = g_w, slot
                    else:
                        g_up, up_slot = g_w, slot
                    gu_seq += 1
                wy("tensor", v_yT)
                w("tensor", "u%d" % gate_slot, g_gate)
                tg = {}
                for n in range(2):
                    w("tensor", "s", st["gate_bank_s"][n])
                    for k in range(KT):
                        go("tensor", lambda e, _n=n, _k=k, _sl=gate_slot, _cc=cc: e.matmul(
                            PS[:, _n * 512:_n * 512 + 512],
                            RING[:, R_GU + _sl * 1024 + _k * 128: R_GU + _sl * 1024 + _k * 128 + 128],
                            yT[:, _k * S + _cc + _n * 512: _k * S + _cc + _n * 512 + 512],
                            start=(_k == 0), stop=(_k == KT - 1)))
                    tg[n] = mark("tensor", "t")
                st["gu_slot_t"][gate_slot] = tg[1]
                s_si = {}
                for n in range(2):
                    w("scalar", "t", tg[n])
                    w("scalar", "v", st["sg_slot_v"][n])
                    s_si[n] = go("scalar", lambda e, _n=n: e.activation(
                        sgb[:, _n * 512:_n * 512 + 512], PS[:, _n * 512:_n * 512 + 512],
                        AF.Silu, scale=CSILU), sem="s")
                    st["gate_bank_s"][n] = s_si[n]
                w("tensor", "u%d" % up_slot, g_up)
                tu = {}
                for n in range(2):
                    w("tensor", "v", st["up_bank_v"][n])
                    for k in range(KT):
                        go("tensor", lambda e, _n=n, _k=k, _sl=up_slot, _cc=cc: e.matmul(
                            PS[:, 1024 + _n * 512: 1024 + _n * 512 + 512],
                            RING[:, R_GU + _sl * 1024 + _k * 128: R_GU + _sl * 1024 + _k * 128 + 128],
                            yT[:, _k * S + _cc + _n * 512: _k * S + _cc + _n * 512 + 512],
                            start=(_k == 0), stop=(_k == KT - 1)))
                    tu[n] = mark("tensor", "t")
                st["gu_slot_t"][up_slot] = tu[1]
                for n in range(2):
                    w("vector", "s", s_si[n])
                    w("vector", "t", tu[n])
                    vh = go("vector", lambda e, _n=n, _ht=ht: e.tensor_mul(
                        ARN[:, hbase + _ht * 1024 + _n * 512: hbase + _ht * 1024 + _n * 512 + 512],
                        sgb[:, _n * 512:_n * 512 + 512],
                        PS[:, 1024 + _n * 512: 1024 + _n * 512 + 512]), sem="v")
                    st["up_bank_v"][n] = vh
                    st["sg_slot_v"][n] = vh
                v_h[ht] = vh
            if not do_down:
                v_last = v_h[nht - 1]
                continue
            for dt in range(8):
                slot = dn_seq % 2
                dn_seq += 1
                w("gpsimd", "t", st["dn_slot_t"][slot])
                g_dn = dma("gpsimd", "n%d" % slot,
                           RING[:, R_DN + slot * 3072: R_DN + slot * 3072 + 3072],
                           WD[dt])
                for sc2 in range(2):
                    bank = (dt * 2 + sc2) % 3
                    w("tensor", "v", v_h[HT - 1])
                    w("tensor", "v", st["dn_bank_v"][bank])
                    w("tensor", "n%d" % slot, g_dn)
                    for khg in range(HT):
                        go("tensor", lambda e, _b=bank, _s=slot, _khg=khg,
                           _dt=dt, _sc2=sc2, _cc=cc: e.matmul(
                            PS[:, 2048 + _b * 512: 2048 + _b * 512 + 512],
                            RING[:, R_DN + _s * 3072 + _khg * 128: R_DN + _s * 3072 + _khg * 128 + 128],
                            ARN[:, A_HT + _khg * 1024 + _sc2 * 512: A_HT + _khg * 1024 + _sc2 * 512 + 512],
                            start=(_khg == 0), stop=(_khg == HT - 1)))
                    t_dn = mark("tensor", "t")
                    if sc2 == 1:
                        st["dn_slot_t"][slot] = t_dn
                    w("vector", "t", t_dn)
                    v_last = go("vector", lambda e, _b=bank, _dt=dt, _sc2=sc2, _cc=cc:
                                e.scalar_tensor_tensor(
                                    rT[:, _dt * S + _cc + _sc2 * 512: _dt * S + _cc + _sc2 * 512 + 512],
                                    PS[:, 2048 + _b * 512: 2048 + _b * 512 + 512],
                                    CDOWN,
                                    rT[:, _dt * S + _cc + _sc2 * 512: _dt * S + _cc + _sc2 * 512 + 512],
                                    ALU.mult, ALU.add), sem="v")
                    st["dn_bank_v"][bank] = v_last
                    if sc == nsc - 1 and sc2 == 1:
                        st["dt_done_v"][dt] = v_last
        return v_last

    attn_state = {
        "qk_slot_t": [0] * 4, "wv_slot_t": [0, 0],
        "p_slot_t": [0] * 4, "sc_bank_s": [0, 0],
        "av_free_v": [0, 0], "obuf_slot_d": [0, 0],
        "fill_v": 0,
        "sg_free_v": 0,
        "sqn_free_t": [0, 0],
        "vext_free_t": [0, 0],
        "last_av_t": 0,
        "pool_banks": (3584,), "pool_i": 0, "pool_hist": {},
        "qk_done_v": {}, "vext_done_v": {}, "negc_v": {}, "d_ot_map": {},
        "fcin_seq": 0, "fcin_slot_t": [0] * 12, "fcin_d": {}, "g_fc": {},
        "fc_t_last": 0,
    }

    def emit_attn(v_yT, guard=None):
        st = attn_state
        if guard:
            st["fill_v"] = max(st["fill_v"], guard.get("v", 0))
            st["av_free_v"] = [max(x, guard.get("v", 0)) for x in st["av_free_v"]]
            st["sc_bank_s"] = [max(x, guard.get("s", 0)) for x in st["sc_bank_s"]]
            st["g_first_s"] = guard.get("s", 0)

        def pool_set(banks):
            st["pool_banks"] = banks
            st["pool_i"] = 0

        def next_bank():
            bb = st["pool_banks"][st["pool_i"] % len(st["pool_banks"])]
            st["pool_i"] += 1
            prev = st["pool_hist"].get(bb)
            w("tensor", "v", prev if prev is not None else st["fill_v"])
            return bb

        def done_bank(bb, v_val):
            st["pool_hist"][bb] = v_val
            st["fill_v"] = max(st["fill_v"], v_val)

        g_qk = {}
        g_wv = {}

        def emit_loads_qk(g):
            for tl in range(4):
                et = (2 * g + tl % 2) if tl < 2 else (8 + 2 * g + tl % 2)
                w("gpsimd", "t", st["qk_slot_t"][tl])
                g_qk[(g, tl)] = dma("gpsimd", "u%d" % tl,
                                    RING[:, R_GU + tl * 1024: R_GU + tl * 1024 + 1024],
                                    WQK[et])

        def emit_loads_wv(g):
            wvs = g % 2
            w("gpsimd", "t", st["wv_slot_t"][wvs])
            g_wv[g] = dma("gpsimd", "n%d" % wvs,
                          RING[:, R_DN + wvs * 2048: R_DN + wvs * 2048 + 2048], WV[g])

        def build_fill_items(g):
            par = g % 2
            qkb = A_QK[par]
            vxb = A_VX[par]
            wvs = g % 2
            items = []
            sq_v = {}

            def qk_part(tl, ch, last_of_tile, first_s=None):
                def f():
                    w("tensor", "u%d" % tl, g_qk[(g, tl)])
                    wy("tensor", v_yT)
                    bb = next_bank()
                    if first_s is not None:
                        w("tensor", "s", first_s)
                    for k in range(KT):
                        go("tensor", lambda e, _tl=tl, _ch=ch, _k=k, _bb=bb: e.matmul(
                            PS[:, _bb:_bb + 512],
                            RING[:, R_GU + _tl * 1024 + _k * 128: R_GU + _tl * 1024 + _k * 128 + 128],
                            yT[:, _k * S + _ch * 512: _k * S + _ch * 512 + 512],
                            start=(_k == 0), stop=(_k == KT - 1)))
                    t_m = mark("tensor", "t")
                    if last_of_tile:
                        st["qk_slot_t"][tl] = t_m
                    et = (2 * g + tl % 2) if tl < 2 else (8 + 2 * g + tl % 2)
                    w("vector", "t", t_m)
                    w("vector", "f0", d_bqk)
                    v_e = go("vector", lambda e, _tl=tl, _ch=ch, _et=et, _bb=bb: e.tensor_scalar(
                        ARN[:, qkb + _tl * 2048 + _ch * 512: qkb + _tl * 2048 + _ch * 512 + 512],
                        PS[:, _bb:_bb + 512],
                        bqk[:, _et:_et + 1], None, ALU.add), sem="v")
                    done_bank(bb, v_e)
                    st["qk_done_v"][g] = v_e
                return f

            def sq_item(tl):
                def f():
                    slot = tl % 2
                    w("vector", "t", st["sqn_free_t"][slot])
                    w("vector", "v", st["qk_done_v"][g])
                    sq_v[tl] = go("vector", lambda e, _tl=tl, _sl=slot: e.tensor_mul(
                        ARN[:, A_SQN + _sl * 2048: A_SQN + _sl * 2048 + 2048],
                        ARN[:, qkb + _tl * 2048: qkb + _tl * 2048 + 2048],
                        ARN[:, qkb + _tl * 2048: qkb + _tl * 2048 + 2048]), sem="v")
                return f

            def norm_item(tl, ch, idx):
                def f():
                    slot = tl % 2
                    w("tensor", "v", sq_v[tl])
                    bb = next_bank()
                    go("tensor", lambda e, _sl=slot, _ch=ch, _bb=bb: e.matmul(
                        PS[0:1, _bb:_bb + 512], ones[:, 0:1],
                        ARN[:, A_SQN + _sl * 2048 + _ch * 512: A_SQN + _sl * 2048 + _ch * 512 + 512],
                        start=True, stop=True))
                    t_m = mark("tensor", "t")
                    st["sqn_free_t"][slot] = t_m
                    w("vector", "t", t_m)
                    v_r = go("vector", lambda e, _i=idx, _bb=bb: e.reduce_max(
                        nmx[0:1, _i:_i + 1], PS[0:1, _bb:_bb + 512], axis=AX.X), sem="v")
                    done_bank(bb, v_r)
                return f

            def stab_item():
                def f():
                    go("vector", lambda e: e.reduce_max(
                        nmx[0:1, 0:1], nmx[0:1, 0:16], axis=AX.X))
                    v_nc = go("vector", lambda e, _g=g: e.tensor_scalar(
                        nb16[0:1, _g:_g + 1], nmx[0:1, 0:1], -0.125, None, ALU.mult),
                        sem="v")
                    w("tensor", "v", v_nc)
                    bb = next_bank()
                    go("tensor", lambda e, _g=g, _bb=bb: e.matmul(
                        PS[0:128, _bb:_bb + 1], ones[0:1, 0:128], nb16[0:1, _g:_g + 1],
                        start=True, stop=True))
                    t_m = mark("tensor", "t")
                    w("vector", "t", t_m)
                    v_n = go("vector", lambda e, _g=g, _bb=bb: e.tensor_copy(
                        negc[:, _g:_g + 1], PS[:, _bb:_bb + 1]), sem="v")
                    done_bank(bb, v_n)
                    st["negc_v"][g] = v_n
                return f

            def memset_item():
                def f():
                    w("vector", "t", st["vext_free_t"][par])
                    go("vector", lambda e: e.memset(
                        ARN[:, vxb:vxb + 4160], 1.0), sem="v")
                return f

            def v_part(stl, last):
                def f():
                    w("tensor", "n%d" % wvs, g_wv[g])
                    wy("tensor", v_yT)
                    bb = next_bank()
                    for k in range(KT):
                        go("tensor", lambda e, _k=k, _stl=stl, _bb=bb: e.matmul(
                            PS[:, _bb:_bb + 256],
                            yT[:, _k * S + _stl * 128: _k * S + _stl * 128 + 128],
                            RING[:, R_DN + wvs * 2048 + _k * 256: R_DN + wvs * 2048 + _k * 256 + 256],
                            start=(_k == 0), stop=False))
                    w("tensor", "gv", d_bv)
                    w("tensor", "v", v_ones)
                    go("tensor", lambda e, _stl=stl, _g=g, _bb=bb: e.matmul(
                        PS[:, _bb:_bb + 256],
                        ones[0:1, 0:128], bv16[0:1, _g * 256:_g * 256 + 256],
                        start=False, stop=True))
                    t_m = mark("tensor", "t")
                    if last:
                        st["wv_slot_t"][wvs] = t_m
                    w("vector", "t", t_m)
                    v_e = go("vector", lambda e, _stl=stl, _bb=bb: e.tensor_copy(
                        ARN[:, vxb:vxb + 4160]
                        .rearrange("p (h j c) -> p h j c", h=4, j=16)[:, :, _stl, 0:64],
                        PS[:, _bb:_bb + 256].rearrange("p (h c) -> p h c", h=4)), sem="v")
                    done_bank(bb, v_e)
                    st["vext_done_v"][g] = v_e
                return f

            first_s = st.pop("g_first_s", None)
            for tl in range(4):
                for ch in range(4):
                    fs = first_s if (tl == 0 and ch == 0) else None
                    items.append(qk_part(tl, ch, ch == 3, fs))
                items.append(sq_item(tl))
                for ch in range(4):
                    items.append(norm_item(tl, ch, tl * 4 + ch))
            items.append(stab_item())
            items.append(memset_item())
            for stl in range(16):
                items.append(v_part(stl, stl == 15))
            return items

        def emit_fcin_load(sc4, ks):
            def f():
                ic = sc4 // 2
                for k in ks:
                    slot = st["fcin_seq"] % 12
                    st["fcin_seq"] += 1
                    w("sync", "o%d" % ic, st["d_ot_map"][(2 * k + 1, ic)])
                    w("sync", "t", st["fcin_slot_t"][slot])
                    d_in = dma("sync", "f%d" % (slot % 4),
                               ARN[:, slot * 512: slot * 512 + 512],
                               OT[k][:, sc4 * 512: sc4 * 512 + 512])
                    st["fcin_d"][(k, sc4)] = (d_in, slot)
            return f

        def emit_fc_chunk(sc4, et, ks, final, v_yT=None):
            def f():
                bb = next_bank()
                w("tensor", "fcw", st["g_fc"][et])
                for i, k in enumerate(ks):
                    d_in, slot = st["fcin_d"][(k, sc4)]
                    w("tensor", "f%d" % (slot % 4), d_in)
                    go("tensor", lambda e, _et=et, _k=k, _sl=slot, _bb=bb,
                       _st=(i == 0), _sp=(i == len(ks) - 1): e.matmul(
                        PS[:, _bb:_bb + 512],
                        RING[:, _et * 1024 + _k * 128: _et * 1024 + _k * 128 + 128],
                        ARN[:, _sl * 512: _sl * 512 + 512],
                        start=_st, stop=_sp))
                t_m = mark("tensor", "t")
                st["fc_t_last"] = t_m
                if et == 7:
                    for k in ks:
                        st["fcin_slot_t"][st["fcin_d"][(k, sc4)][1]] = t_m
                w("vector", "t", t_m)
                if not final:
                    v_e = go("vector", lambda e, _et=et, _sc4=sc4, _bb=bb:
                             e.scalar_tensor_tensor(
                                 rT[:, _et * S + _sc4 * 512: _et * S + _sc4 * 512 + 512],
                                 PS[:, _bb:_bb + 512], 1.0,
                                 rT[:, _et * S + _sc4 * 512: _et * S + _sc4 * 512 + 512],
                                 ALU.mult, ALU.add), sem="v")
                else:
                    w("vector", "f0", d_bfc)
                    if v_yT and v_yT.get("gv"):
                        w("vector", "gv", v_yT["gv"])
                    sl2 = et % 2
                    v_e = go("vector", lambda e, _et=et, _sl=sl2, _sc4=sc4, _bb=bb:
                             e.scalar_tensor_tensor(
                                 tfb[:, _sl * 512: _sl * 512 + 512],
                                 PS[:, _bb:_bb + 512], bfc[:, _et:_et + 1],
                                 yT[:, _et * S + _sc4 * 512: _et * S + _sc4 * 512 + 512],
                                 ALU.add, ALU.add), sem="v")
                    v_e = go("vector", lambda e, _et=et, _sl=sl2, _sc4=sc4: e.tensor_add(
                        rT[:, _et * S + _sc4 * 512: _et * S + _sc4 * 512 + 512],
                        rT[:, _et * S + _sc4 * 512: _et * S + _sc4 * 512 + 512],
                        tfb[:, _sl * 512: _sl * 512 + 512]), sem="v")
                done_bank(bb, v_e)
                st["fc_v_last"] = v_e
            return f

        def build_fc_fill_items():
            for et in range(8):
                if et < 4:
                    w("gpsimd", "t", st["qk_slot_t"][et])
                w("gpsimd", "t", st["wv_slot_t"][0])
                w("gpsimd", "t", st["wv_slot_t"][1])
                st["g_fc"][et] = dma("gpsimd", "fcw",
                                     RING[:, et * 1024:(et + 1) * 1024], WFC[et])
            items = [emit_fcin_load(0, range(6)), emit_fcin_load(1, range(6))]
            for sc4 in range(4):
                if sc4 + 2 < 4:
                    items.append(emit_fcin_load(sc4 + 2, range(6)))
                for et in range(8):
                    items.append(emit_fc_chunk(sc4, et, list(range(6)), False))
            return items

        def emit_core(g):
            par = g % 2
            qkb = A_QK[par]
            vxb = A_VX[par]
            units = [(hl, ic, j) for hl in range(GH) for ic in range(2)
                     for j in range(16)]
            fill = build_fill_items(g + 1) if g + 1 < NG else build_fc_fill_items()
            fi = [0]
            qk_emitted = [False]
            v_emitted = [False]

            def drip_fill(target_idx):
                want = (target_idx * len(fill)) // (len(units) - 8) if fill else 0
                while fi[0] < min(want, len(fill)):
                    fill[fi[0]]()
                    fi[0] += 1
                    if fill and fi[0] == 36 and not qk_emitted[0] and g + 2 < NG:
                        emit_loads_qk(g + 2)
                        qk_emitted[0] = True
                    if fill and fi[0] == len(fill) and not v_emitted[0] and g + 2 < NG:
                        emit_loads_wv(g + 2)
                        v_emitted[0] = True

            t_sc = {}

            def emit_scores(idx):
                hl, ic, j = units[idx]
                sbk = idx % 2
                w("tensor", "s", st["sc_bank_s"][sbk])
                w("tensor", "v", st.get("pro_v", 0))
                w("tensor", "v", st["qk_done_v"][g])
                qtile, ktile, roff = hl // 2, 2 + hl // 2, (hl % 2) * 64
                icc = ic * 1024
                for n in range(2):
                    go("tensor", lambda e, _sbk=sbk, _n=n, _qt=qtile, _kt=ktile,
                       _ro=roff, _j=j, _icc=icc: e.matmul(
                        PS[:, _sbk * 1024 + _n * 512: _sbk * 1024 + _n * 512 + 512],
                        ARN[_ro:_ro + 64, qkb + _kt * 2048 + _j * 128: qkb + _kt * 2048 + _j * 128 + 128],
                        ARN[_ro:_ro + 64, qkb + _qt * 2048 + _icc + _n * 512: qkb + _qt * 2048 + _icc + _n * 512 + 512],
                        start=True, stop=True))
                t_sc[idx] = mark("tensor", "t")

            def epilogue_a(pend):
                h, ic, t_av = pend
                v_ev = None
                for n in range(2):
                    w("vector", "t", t_av)
                    w("vector", "v", st["sg_free_v"])
                    v_ev = go("vector", lambda e, _n=n: e.tensor_copy(
                        sgb[0:65, _n * 512: _n * 512 + 512],
                        PS[0:65, 2048 + _n * 512: 2048 + _n * 512 + 512]), sem="v")
                    st["av_free_v"][n] = v_ev
                st["last_ev_v"] = v_ev

            def emit_recip():
                w("scalar", "v", st["last_ev_v"])
                go("scalar", lambda e: e.activation(
                    tfb[64:65, 0:1024], sgb[64:65, 0:1024], AF.Ln))
                s_rv = go("scalar", lambda e: e.activation(
                    rvb[64:65, 0:1024], tfb[64:65, 0:1024], AF.Exp, scale=-1.0),
                    sem="s")
                return s_rv

            def flush_epilogue(pend, s_rv):
                h, ic, t_av = pend
                icc = ic * 1024
                oslot = (h * 2 + ic) % 2
                v_mu = None
                for n in range(2):
                    bb = next_bank()
                    w("tensor", "s", s_rv)
                    go("tensor", lambda e, _n=n, _bb=bb: e.matmul(
                        PS[0:64, _bb:_bb + 512], ones[64:65, 0:64],
                        rvb[64:65, _n * 512:_n * 512 + 512],
                        start=True, stop=True))
                    t_b = mark("tensor", "t")
                    w("vector", "t", t_b)
                    if n == 0:
                        w("vector", "o%d" % oslot, st["obuf_slot_d"][oslot])
                    v_mu = go("vector", lambda e, _os=oslot, _n=n, _bb=bb: e.tensor_mul(
                        obuf[0:64, _os * 1024 + _n * 512: _os * 1024 + _n * 512 + 512],
                        sgb[0:64, _n * 512:_n * 512 + 512],
                        PS[0:64, _bb:_bb + 512]), sem="v")
                    done_bank(bb, v_mu)
                st["sg_free_v"] = v_mu
                w("sync", "v", v_mu)
                d_ot = dma("sync", "o%d" % oslot,
                           OT[h // 2][(h % 2) * 64:(h % 2) * 64 + 64, icc:icc + 1024],
                           obuf[0:64, oslot * 1024: oslot * 1024 + 1024])
                st["obuf_slot_d"][oslot] = d_ot
                st["d_ot_map"][(h, ic)] = d_ot

            emit_scores(0)
            emit_scores(1)
            pending = None
            pend_srv = None
            for idx, (hl, ic, j) in enumerate(units):
                h = g * GH + hl
                sbk = idx % 2
                pslot = idx % 4
                w("scalar", "t", t_sc[idx])
                w("scalar", "t", st["p_slot_t"][pslot])
                w("scalar", "v", st["negc_v"][g])
                s_exp = go("scalar", lambda e, _sbk=sbk, _ps=pslot, _g=g: e.activation(
                    ARN[:, A_P + _ps * 1024: A_P + _ps * 1024 + 1024],
                    PS[:, _sbk * 1024: _sbk * 1024 + 1024],
                    AF.Exp, bias=negc[:, _g:_g + 1], scale=0.125), sem="s")
                st["sc_bank_s"][sbk] = s_exp
                if idx + 2 < len(units):
                    emit_scores(idx + 2)
                if pending is not None:
                    if pend_srv is None and idx - pend_idx >= 2:
                        pend_srv = emit_recip()
                    elif pend_srv is not None and idx - pend_idx >= 3:
                        flush_epilogue(pending, pend_srv)
                        pending = None
                        pend_srv = None
                w("tensor", "s", s_exp)
                if j == 0:
                    w("tensor", "v", st["vext_done_v"][g])
                for n in range(2):
                    if j == 0:
                        w("tensor", "v", st["av_free_v"][n])
                    go("tensor", lambda e, _hl=hl, _j=j, _n=n, _ps=pslot, _vx=vxb: e.matmul(
                        PS[0:65, 2048 + _n * 512: 2048 + _n * 512 + 512],
                        ARN[:, _vx + (_hl * 16 + _j) * 65: _vx + (_hl * 16 + _j) * 65 + 65],
                        ARN[:, A_P + _ps * 1024 + _n * 512: A_P + _ps * 1024 + _n * 512 + 512],
                        start=(_j == 0), stop=(_j == 15)))
                t_av = mark("tensor", "t")
                st["p_slot_t"][pslot] = t_av
                st["last_av_t"] = t_av
                if j == 15:
                    pending = (h, ic, t_av)
                    epilogue_a(pending)
                    pend_srv = None
                    pend_idx = idx
                drip_fill(idx)
            drip_fill(len(units))
            while fi[0] < len(fill):
                fill[fi[0]]()
                fi[0] += 1
            if fill and g + 2 < NG and not v_emitted[0]:
                emit_loads_wv(g + 2)
            if pend_srv is None:
                pend_srv = emit_recip()
            flush_epilogue(pending, pend_srv)
            st["vext_free_t"][par] = st["last_av_t"]

        emit_loads_qk(0)
        emit_loads_wv(0)
        pool_set((0, 512, 1024, 1536, 2048, 2560))
        for it in build_fill_items(0):
            it()
        emit_loads_qk(1)
        emit_loads_wv(1)
        pro_v = CTR["v"]
        st["sc_bank_s"] = [max(x, 0) for x in st["sc_bank_s"]]
        st["av_free_v"] = [max(x, pro_v) for x in st["av_free_v"]]
        st["pro_v"] = pro_v
        pool_set((3072, 3584))
        st["emit_fcin_load"] = emit_fcin_load
        st["emit_fc_chunk"] = emit_fc_chunk
        for g in range(NG):
            emit_core(g)
        return [CTR["o0"], CTR["o1"]]

    def emit_fc(v_yT, guard=None):
        st = attn_state
        for sc4 in range(4):
            st["emit_fcin_load"](sc4, (6, 7))()
        for sc4 in range(4):
            for et in range(8):
                st["emit_fc_chunk"](sc4, et, [6, 7], True, v_yT)()
        t_last = st["fc_t_last"]
        ffn_state["gu_slot_t"] = [t_last] * 4
        ffn_state["dn_slot_t"] = [t_last] * 2
        return st["fc_v_last"]

    v_r = None
    v_y = emit_ln(dep_v=None, dep_d=d_x)
    if stop_phase > 1:
        v_r = emit_ffn(v_y, guard=snap(), **(ffn_kw or {}))
        attn_state["qk_slot_t"] = list(ffn_state["gu_slot_t"])[:4]
        dn_last = max(ffn_state["dn_slot_t"])
        attn_state["wv_slot_t"] = [dn_last, dn_last]
    if stop_phase > 2:
        v_y = emit_ln(dep_v=v_r, guard=snap())
    if stop_phase > 3:
        d_ot_all = emit_attn(v_y, guard=snap())
    if stop_phase > 4:
        v_r = emit_fc(v_y, guard=snap())
    if stop_phase > 5:
        v_y = emit_ln(dep_v=v_r, guard=snap())
    if stop_phase > 6:
        v_r = emit_ffn(v_y, guard=snap())

    if dbg == "yT":
        for k in range(KT):
            w("gpsimd", "v", CTR["v"])
            dma("gpsimd", "gv", OUT[k], yT[:, k * S:(k + 1) * S])
        w("sync", "gv", CTR["gv"])
    else:
        for k in range(KT):
            w("sync", "v", ffn_state["dt_done_v"].get(k, CTR["v"]))
            dma("sync", "do", OUT[k], rT[:, k * S:(k + 1) * S])
    for sname in ("do", "d0", "gv", "u0", "u1", "u2", "u3", "n0",
                  "n1", "fcw", "f0", "f1", "f2", "f3", "o0", "o1"):
        w("sync", sname, CTR[sname])

    with nc.Block() as block:
        block.sync(lambda e: [f(e) for f in OPS["sync"]])
        block.gpsimd(lambda e: [f(e) for f in OPS["gpsimd"]])
        block.vector(lambda e: [f(e) for f in OPS["vector"]])
        block.scalar(lambda e: [f(e) for f in OPS["scalar"]])
        block.tensor(lambda e: [f(e) for f in OPS["tensor"]])
    es.close()
    return nc


def _prep(inputs):
    x = np.asarray(inputs["L0_attn_x"], dtype=np.float32)
    qkv_w = np.asarray(inputs["L0_attn_qkv_w"], dtype=np.float32)
    qkv_b = np.asarray(inputs["L0_attn_qkv_b"], dtype=np.float32)
    fc_w = np.asarray(inputs["L0_attn_fc_w"], dtype=np.float32)
    fc_b = np.asarray(inputs["L0_attn_fc_b"], dtype=np.float32)
    wg = np.asarray(inputs["L0_ffn_w_gate"], dtype=np.int32)
    wu = np.asarray(inputs["L0_ffn_w_up"], dtype=np.int32)
    wd = np.asarray(inputs["L0_ffn_w_down"], dtype=np.int32)

    idx_q = (np.arange(16)[:, None] * 192 + np.arange(64)[None, :]).ravel()
    perm = np.concatenate([idx_q, idx_q + 64, idx_q + 128])
    wT = np.ascontiguousarray(qkv_w[perm].T)
    b_perm = qkv_b[perm]

    wqk = np.ascontiguousarray(
        wT[:, :2048].reshape(KT, 128, 16, 128).transpose(2, 1, 0, 3))
    wv = np.ascontiguousarray(
        wT[:, 2048:].reshape(KT, 128, NG, 256).transpose(2, 1, 0, 3))
    wfc = np.ascontiguousarray(
        fc_w.T.reshape(KT, 128, 8, 128).transpose(2, 1, 0, 3))
    wgt = np.ascontiguousarray(
        wg.T.reshape(KT, 128, HT, 128).transpose(2, 1, 0, 3))
    wut = np.ascontiguousarray(
        wu.T.reshape(KT, 128, HT, 128).transpose(2, 1, 0, 3))
    wdt = np.ascontiguousarray(
        wd.T.reshape(HT, 128, 8, 128).transpose(2, 1, 0, 3))
    bqk = np.ascontiguousarray(b_perm[:2048].reshape(16, 128).T)
    bv = np.ascontiguousarray(b_perm[2048:].reshape(1, 1024))
    bfc = np.ascontiguousarray(fc_b.reshape(8, 128).T)

    shared = {"wqk": wqk, "wv": wv, "wfc": wfc, "wg": wgt, "wu": wut,
              "wd": wdt, "bqk": bqk, "bv": bv, "bfc": bfc}
    in_maps = []
    for b in range(8):
        m = dict(shared)
        m["xt"] = np.ascontiguousarray(x[b].T).reshape(KT, 128, S)
        in_maps.append(m)
    return in_maps


def kernel(**inputs):
    in_maps = _prep(inputs)
    nc = build_nc()
    res = run_bass_kernel_spmd(nc, in_maps, core_ids=list(range(8)))
    out = np.empty((8, S, D), dtype=np.float32)
    for b in range(8):
        ot = np.asarray(res.results[b]["out"], dtype=np.float32)
        out[b] = ot.reshape(D, S).T
    return out


# version 21
# speedup vs baseline: 1.0480x; 1.0480x over previous
import numpy as np
import ml_dtypes
from contextlib import ExitStack

import concourse.bass as bass
import concourse.mybir as mybir
from concourse.bass_utils import run_bass_kernel_spmd

F32 = mybir.dt.float32
BF16 = mybir.dt.bfloat16
I32 = mybir.dt.int32
AF = mybir.ActivationFunctionType
ALU = mybir.AluOpType
AX = mybir.AxisListType

D = 1024
S = 2048
HID = 3072
NH = 16
KT = 8
HT = 24
NG = 4
GH = 4
CDOWN = 0.5 / (127.0 * 127.0)
CSILU = 1.0 / 127.0

ENGS = ("sync", "gpsimd", "vector", "scalar", "tensor")

A_XB = 0
A_SQ = 8192
A_TB = 16384
A_HT = 0
A_QK = (0, 8192)
A_P = 16384
A_VX = (20480, 24640)
A_SQN = 28800
R_GU = 0
R_DN = 4096


def build_nc(stop_phase=99, dbg=None, ffn_kw=None):
    nc = bass.Bass("TRN2", target_bir_lowering=False,
                   detect_race_conditions=False)

    XT = nc.dram_tensor("xt", [KT, 128, S], F32, kind="ExternalInput").ap()
    WQK = nc.dram_tensor("wqk", [16, 128, KT, 128], F32, kind="ExternalInput").ap()
    WV = nc.dram_tensor("wv", [NG, 128, KT, 256], F32, kind="ExternalInput").ap()
    WFC = nc.dram_tensor("wfc", [8, 128, KT, 128], F32, kind="ExternalInput").ap()
    WG = nc.dram_tensor("wg", [HT, 128, KT, 128], I32, kind="ExternalInput").ap()
    WU = nc.dram_tensor("wu", [HT, 128, KT, 128], I32, kind="ExternalInput").ap()
    WD = nc.dram_tensor("wd", [8, 128, HT, 128], I32, kind="ExternalInput").ap()
    BQK = nc.dram_tensor("bqk", [128, 16], F32, kind="ExternalInput").ap()
    BV = nc.dram_tensor("bv", [1, D], F32, kind="ExternalInput").ap()
    BFC = nc.dram_tensor("bfc", [128, 8], F32, kind="ExternalInput").ap()
    OUT = nc.dram_tensor("out", [KT, 128, S], F32, kind="ExternalOutput").ap()
    OT = nc.dram_tensor("ot", [KT, 128, S], BF16, kind="Internal").ap()

    es = ExitStack()
    sbt = lambda name, shape, dt: es.enter_context(nc.sbuf_tensor(name, shape, dt))

    rT = sbt("rT", [128, KT * S], F32)
    yT = sbt("yT", [128, KT * S], BF16)
    ARN = sbt("arn", [128, 32896], BF16)
    RING = sbt("ring", [128, 10240], BF16)
    obuf = sbt("obuf", [128, 2 * 1024], BF16)
    sgb = sbt("sgb", [128, 2 * 512], F32)
    tfb = sbt("tfb", [128, 2 * 512], F32)
    fcin = obuf
    rvb = sbt("rvb", [128, 1024], BF16)
    ones = sbt("ones", [128, 512], BF16)
    rowA = sgb
    rowB = sbt("rowB", [1, 1024], F32)
    rowC = tfb
    rB16 = sbt("rB16", [1, 1024], BF16)
    rE16 = sbt("rE16", [1, 1024], BF16)
    bqk = sbt("bqk_sb", [128, 16], F32)
    bfc = sbt("bfc_sb", [128, 8], F32)
    bv16 = sbt("bv16", [1, D], BF16)
    negc = sbt("negc", [128, NG], F32)
    nmx = sbt("nmx", [1, 16], F32)
    nb16 = sbt("nb16", [1, 16], BF16)
    epsr = sbt("epsr", [1, 1], F32)
    PS = es.enter_context(nc.psum_tensor("ps", [128, 4096], F32))

    SEMNAMES = ("d0", "gv", "u0", "u1", "u2", "u3", "n0", "n1",
                "fcw", "f0", "f1", "f2", "f3", "o0", "o1", "do", "t", "v", "s")
    sems = {k: es.enter_context(nc.semaphore("sem_" + k)) for k in SEMNAMES}
    OPS = {e: [] for e in ENGS}
    CTR = {k: 0 for k in SEMNAMES}
    WAITED = {e: {k: 0 for k in SEMNAMES} for e in ENGS}

    def w(eng, sem, val):
        if not val or val <= WAITED[eng][sem]:
            return
        WAITED[eng][sem] = val
        OPS[eng].append(lambda e, _s=sems[sem], _v=val: e.wait_ge(_s, _v))

    def go(eng, fn, sem=None, inc=1):
        if sem is None:
            OPS[eng].append(fn)
            return None
        CTR[sem] += inc
        OPS[eng].append(lambda e, _f=fn, _s=sems[sem], _i=inc: _f(e).then_inc(_s, _i))
        return CTR[sem]

    def mark(eng, sem):
        CTR[sem] += 1
        prev = OPS[eng].pop()
        OPS[eng].append(lambda e, _p=prev, _s=sems[sem]: _p(e).then_inc(_s, 1))
        return CTR[sem]

    def dma(eng, sem, out_ap, in_ap):
        return go(eng, lambda e, _o=out_ap, _i=in_ap: e.dma_start(_o, _i),
                  sem=sem, inc=16)

    def wy(eng, v_yT, half=None):
        vh = v_yT.get("vh")
        if half is not None and vh and vh[half]:
            w(eng, "v", vh[half])
        else:
            w(eng, "v", v_yT["v"])
        if v_yT.get("gv"):
            w(eng, "gv", v_yT["gv"])

    XSEM = ("d0", "f1", "f2", "f3")
    d_xk = [(XSEM[k % 4], dma("sync", XSEM[k % 4], rT[:, k * S:(k + 1) * S],
                              XT[k])) for k in range(KT)]
    d_bqk = dma("sync", "f0", bqk[:], BQK)
    d_bfc = dma("sync", "f0", bfc[:], BFC)
    d_x = d_xk
    d_bv = dma("gpsimd", "gv", bv16[:], BV)
    v_ones = go("vector", lambda e: e.memset(ones[:], 1.0), sem="v")
    v_eps = go("vector", lambda e: e.memset(epsr[:], 1e-5), sem="v")

    def snap():
        return dict(CTR)

    def emit_ln(dep_v=None, dep_d=None, guard=None):
        v_done = None
        g_done = None
        v_half = [None, None]
        gd = (guard or {}).get("v", 0)
        prev_row_v = (guard or {}).get("v")
        prev_cp_v = (guard or {}).get("v")
        if guard:
            w("tensor", "s", guard.get("s"))
        prev_s = None
        for c in range(2):
            cc = c * 1024
            v_cp = {}
            cp_eng = {}
            s_sq = {}
            for k in range(KT):
                eng = "vector"
                cp_eng[k] = eng
                if dep_d: w(eng, dep_d[k][0], dep_d[k][1])
                w(eng, "v", v_ones)
                if dep_v: w(eng, "v", dep_v)
                if gd: w(eng, "v", gd)
                if prev_s: w(eng, "s", prev_s)
                v_cp[k] = go(eng, lambda e, _k=k, _cc=cc: e.tensor_copy(
                    ARN[:, A_XB + _k * 1024: A_XB + _k * 1024 + 1024],
                    rT[:, _k * S + _cc: _k * S + _cc + 1024]),
                    sem=("v" if eng == "vector" else "gv"))
            for k in range(KT):
                w("scalar", "v" if cp_eng[k] == "vector" else "gv", v_cp[k])
                s_sq[k] = go("scalar", lambda e, _k=k: e.activation(
                    ARN[:, A_SQ + _k * 1024: A_SQ + _k * 1024 + 1024],
                    ARN[:, A_XB + _k * 1024: A_XB + _k * 1024 + 1024],
                    AF.Square), sem="s")
            if prev_row_v: w("tensor", "v", prev_row_v)
            for k in range(KT):
                w("tensor", "v" if cp_eng[k] == "vector" else "gv", v_cp[k])
                for n in range(2):
                    go("tensor", lambda e, _k=k, _n=n: e.matmul(
                        PS[0:1, _n * 512:_n * 512 + 512], ones[:, 0:1],
                        ARN[:, A_XB + _k * 1024 + _n * 512: A_XB + _k * 1024 + _n * 512 + 512],
                        start=(_k == 0), stop=(_k == KT - 1)))
            t_sums = mark("tensor", "t")
            for k in range(KT):
                w("tensor", "s", s_sq[k])
                for n in range(2):
                    go("tensor", lambda e, _k=k, _n=n: e.matmul(
                        PS[0:1, 1024 + _n * 512: 1024 + _n * 512 + 512], ones[:, 0:1],
                        ARN[:, A_SQ + _k * 1024 + _n * 512: A_SQ + _k * 1024 + _n * 512 + 512],
                        start=(_k == 0), stop=(_k == KT - 1)))
            t_sq = mark("tensor", "t")
            v_row = {}
            s_row = {}
            for n in range(2):
                sl = slice(n * 512, n * 512 + 512)
                w("vector", "t", t_sums)
                w("vector", "t", t_sq)
                go("vector", lambda e, _sl=sl: e.tensor_scalar(
                    rowA[0:1, _sl], PS[0:1, _sl], 1.0 / D, None, ALU.mult))
                go("vector", lambda e, _sl=sl, _n=n: e.tensor_scalar(
                    rowB[0:1, _sl], PS[0:1, 1024 + _n * 512:1024 + _n * 512 + 512],
                    1.0 / D, None, ALU.mult))
                go("vector", lambda e, _sl=sl: e.tensor_mul(
                    rowC[0:1, _sl], rowA[0:1, _sl], rowA[0:1, _sl]))
                v_row[n] = go("vector", lambda e, _sl=sl: e.tensor_sub(
                    rowB[0:1, _sl], rowB[0:1, _sl], rowC[0:1, _sl]), sem="v")
            for n in range(2):
                sl = slice(n * 512, n * 512 + 512)
                w("scalar", "v", v_row[n])
                w("scalar", "v", v_eps)
                go("scalar", lambda e, _sl=sl: e.activation(
                    rowB[0:1, _sl], rowB[0:1, _sl], AF.Ln, bias=epsr[0:1, 0:1]))
                s_row[n] = go("scalar", lambda e, _sl=sl: e.activation(
                    rB16[0:1, _sl], rowB[0:1, _sl], AF.Exp, scale=-0.5), sem="s")
            prev_s = s_row[1]
            v_e = {}
            for n in range(2):
                sl = slice(n * 512, n * 512 + 512)
                w("vector", "s", s_row[n])
                v_e[n] = go("vector", lambda e, _sl=sl: e.tensor_mul(
                    rE16[0:1, _sl], rowA[0:1, _sl], rB16[0:1, _sl]), sem="v")
            prev_row_v = v_e[1]
            if prev_cp_v: w("tensor", "v", prev_cp_v)
            for n in range(2):
                w("tensor", "s", s_row[n])
                w("tensor", "v", v_e[n])
                go("tensor", lambda e, _n=n: e.matmul(
                    PS[:, 2048 + _n * 512: 2048 + _n * 512 + 512],
                    ones[0:1, 0:128], rB16[0:1, _n * 512:_n * 512 + 512],
                    start=True, stop=True))
                go("tensor", lambda e, _n=n: e.matmul(
                    PS[:, 3072 + _n * 512: 3072 + _n * 512 + 512],
                    ones[0:1, 0:128], rE16[0:1, _n * 512:_n * 512 + 512],
                    start=True, stop=True))
            t_bc = mark("tensor", "t")
            w("vector", "t", t_bc)
            if g_done:
                w("vector", "gv", g_done)
            go("vector", lambda e: e.tensor_copy(ARN[:, A_TB:A_TB + 1024],
                                                 PS[:, 2048:3072]))
            prev_cp_v = go("vector", lambda e: e.tensor_copy(
                ARN[:, A_TB + 1024:A_TB + 2048], PS[:, 3072:4096]), sem="v")
            for k in range(KT):
                eng = cp_eng[k]
                if eng == "gpsimd":
                    w(eng, "v", prev_cp_v)
                go(eng, lambda e, _k=k: e.tensor_mul(
                    ARN[:, A_XB + _k * 1024: A_XB + _k * 1024 + 1024],
                    ARN[:, A_XB + _k * 1024: A_XB + _k * 1024 + 1024],
                    ARN[:, A_TB:A_TB + 1024]))
                d_ap = go(eng, lambda e, _k=k, _cc=cc: e.tensor_sub(
                    yT[:, _k * S + _cc: _k * S + _cc + 1024],
                    ARN[:, A_XB + _k * 1024: A_XB + _k * 1024 + 1024],
                    ARN[:, A_TB + 1024:A_TB + 2048]),
                    sem=("v" if eng == "vector" else "gv"))
                if eng == "vector":
                    v_done = d_ap
                else:
                    g_done = d_ap
            v_half[c] = v_done
        return {"v": v_done, "gv": g_done, "vh": tuple(v_half)}

    ffn_state = {"gu_slot_t": [0] * 4, "dn_slot_t": [0, 0],
                 "gate_bank_s": [0, 0], "up_bank_v": [0, 0],
                 "dn_bank_v": [0, 0, 0], "sg_slot_v": [0, 0],
                 "dt_done_v": {}}

    def emit_ffn(v_yT, guard=None, nsc=2, nht=HT, do_down=True):
        st = ffn_state
        if guard:
            st["gate_bank_s"] = [max(x, guard.get("s", 0)) for x in st["gate_bank_s"]]
            st["up_bank_v"] = [max(x, guard.get("v", 0)) for x in st["up_bank_v"]]
            st["dn_bank_v"] = [max(x, guard.get("v", 0)) for x in st["dn_bank_v"]]
        v_last = None
        gu_seq = 0
        dn_seq = 0
        for sc in range(nsc):
            cc = sc * 1024
            hbase = A_HT
            v_h = {}
            for ht in range(nht):
                for wt, W_ in ((0, WG), (1, WU)):
                    slot = gu_seq % 4
                    w("gpsimd", "t", st["gu_slot_t"][slot])
                    g_w = dma("gpsimd", "u%d" % slot,
                              RING[:, R_GU + slot * 1024: R_GU + slot * 1024 + 1024],
                              W_[ht])
                    if wt == 0:
                        g_gate, gate_slot = g_w, slot
                    else:
                        g_up, up_slot = g_w, slot
                    gu_seq += 1
                wy("tensor", v_yT, half=sc)
                w("tensor", "u%d" % gate_slot, g_gate)
                tg = {}
                for n in range(2):
                    w("tensor", "s", st["gate_bank_s"][n])
                    for k in range(KT):
                        go("tensor", lambda e, _n=n, _k=k, _sl=gate_slot, _cc=cc: e.matmul(
                            PS[:, _n * 512:_n * 512 + 512],
                            RING[:, R_GU + _sl * 1024 + _k * 128: R_GU + _sl * 1024 + _k * 128 + 128],
                            yT[:, _k * S + _cc + _n * 512: _k * S + _cc + _n * 512 + 512],
                            start=(_k == 0), stop=(_k == KT - 1)))
                    tg[n] = mark("tensor", "t")
                st["gu_slot_t"][gate_slot] = tg[1]
                s_si = {}
                for n in range(2):
                    w("scalar", "t", tg[n])
                    w("scalar", "v", st["sg_slot_v"][n])
                    s_si[n] = go("scalar", lambda e, _n=n: e.activation(
                        sgb[:, _n * 512:_n * 512 + 512], PS[:, _n * 512:_n * 512 + 512],
                        AF.Silu, scale=CSILU), sem="s")
                    st["gate_bank_s"][n] = s_si[n]
                w("tensor", "u%d" % up_slot, g_up)
                tu = {}
                for n in range(2):
                    w("tensor", "v", st["up_bank_v"][n])
                    for k in range(KT):
                        go("tensor", lambda e, _n=n, _k=k, _sl=up_slot, _cc=cc: e.matmul(
                            PS[:, 1024 + _n * 512: 1024 + _n * 512 + 512],
                            RING[:, R_GU + _sl * 1024 + _k * 128: R_GU + _sl * 1024 + _k * 128 + 128],
                            yT[:, _k * S + _cc + _n * 512: _k * S + _cc + _n * 512 + 512],
                            start=(_k == 0), stop=(_k == KT - 1)))
                    tu[n] = mark("tensor", "t")
                st["gu_slot_t"][up_slot] = tu[1]
                for n in range(2):
                    w("vector", "s", s_si[n])
                    w("vector", "t", tu[n])
                    vh = go("vector", lambda e, _n=n, _ht=ht: e.tensor_mul(
                        ARN[:, hbase + _ht * 1024 + _n * 512: hbase + _ht * 1024 + _n * 512 + 512],
                        sgb[:, _n * 512:_n * 512 + 512],
                        PS[:, 1024 + _n * 512: 1024 + _n * 512 + 512]), sem="v")
                    st["up_bank_v"][n] = vh
                    st["sg_slot_v"][n] = vh
                v_h[ht] = vh
            if not do_down:
                v_last = v_h[nht - 1]
                continue
            for dt in range(8):
                slot = dn_seq % 2
                dn_seq += 1
                w("gpsimd", "t", st["dn_slot_t"][slot])
                g_dn = dma("gpsimd", "n%d" % slot,
                           RING[:, R_DN + slot * 3072: R_DN + slot * 3072 + 3072],
                           WD[dt])
                for sc2 in range(2):
                    bank = (dt * 2 + sc2) % 3
                    w("tensor", "v", v_h[HT - 1])
                    w("tensor", "v", st["dn_bank_v"][bank])
                    w("tensor", "n%d" % slot, g_dn)
                    for khg in range(HT):
                        go("tensor", lambda e, _b=bank, _s=slot, _khg=khg,
                           _dt=dt, _sc2=sc2, _cc=cc: e.matmul(
                            PS[:, 2048 + _b * 512: 2048 + _b * 512 + 512],
                            RING[:, R_DN + _s * 3072 + _khg * 128: R_DN + _s * 3072 + _khg * 128 + 128],
                            ARN[:, A_HT + _khg * 1024 + _sc2 * 512: A_HT + _khg * 1024 + _sc2 * 512 + 512],
                            start=(_khg == 0), stop=(_khg == HT - 1)))
                    t_dn = mark("tensor", "t")
                    if sc2 == 1:
                        st["dn_slot_t"][slot] = t_dn
                    w("vector", "t", t_dn)
                    v_last = go("vector", lambda e, _b=bank, _dt=dt, _sc2=sc2, _cc=cc:
                                e.scalar_tensor_tensor(
                                    rT[:, _dt * S + _cc + _sc2 * 512: _dt * S + _cc + _sc2 * 512 + 512],
                                    PS[:, 2048 + _b * 512: 2048 + _b * 512 + 512],
                                    CDOWN,
                                    rT[:, _dt * S + _cc + _sc2 * 512: _dt * S + _cc + _sc2 * 512 + 512],
                                    ALU.mult, ALU.add), sem="v")
                    st["dn_bank_v"][bank] = v_last
                    if sc == nsc - 1 and sc2 == 1:
                        st["dt_done_v"][dt] = v_last
        return v_last

    attn_state = {
        "qk_slot_t": [0] * 4, "wv_slot_t": [0, 0],
        "p_slot_t": [0] * 4, "sc_bank_s": [0, 0],
        "av_free_v": [0, 0], "obuf_slot_d": [0, 0],
        "fill_v": 0,
        "sg_free_v": 0,
        "sqn_free_t": [0, 0],
        "vext_free_t": [0, 0],
        "last_av_t": 0,
        "pool_banks": (3584,), "pool_i": 0, "pool_hist": {},
        "qk_done_v": {}, "vext_done_v": {}, "negc_v": {}, "d_ot_map": {},
        "fcin_seq": 0, "fcin_slot_t": [0] * 12, "fcin_d": {}, "g_fc": {},
        "fc_t_last": 0,
    }

    def emit_attn(v_yT, guard=None):
        st = attn_state
        if guard:
            st["fill_v"] = max(st["fill_v"], guard.get("v", 0))
            st["av_free_v"] = [max(x, guard.get("v", 0)) for x in st["av_free_v"]]
            st["sc_bank_s"] = [max(x, guard.get("s", 0)) for x in st["sc_bank_s"]]
            st["g_first_s"] = guard.get("s", 0)

        def pool_set(banks):
            st["pool_banks"] = banks
            st["pool_i"] = 0

        def next_bank():
            bb = st["pool_banks"][st["pool_i"] % len(st["pool_banks"])]
            st["pool_i"] += 1
            prev = st["pool_hist"].get(bb)
            w("tensor", "v", prev if prev is not None else st["fill_v"])
            return bb

        def done_bank(bb, v_val):
            st["pool_hist"][bb] = v_val
            st["fill_v"] = max(st["fill_v"], v_val)

        g_qk = {}
        g_wv = {}

        def emit_loads_qk(g):
            for tl in range(4):
                et = (2 * g + tl % 2) if tl < 2 else (8 + 2 * g + tl % 2)
                w("gpsimd", "t", st["qk_slot_t"][tl])
                g_qk[(g, tl)] = dma("gpsimd", "u%d" % tl,
                                    RING[:, R_GU + tl * 1024: R_GU + tl * 1024 + 1024],
                                    WQK[et])

        def emit_loads_wv(g):
            wvs = g % 2
            w("gpsimd", "t", st["wv_slot_t"][wvs])
            g_wv[g] = dma("gpsimd", "n%d" % wvs,
                          RING[:, R_DN + wvs * 2048: R_DN + wvs * 2048 + 2048], WV[g])

        def build_fill_items(g):
            par = g % 2
            qkb = A_QK[par]
            vxb = A_VX[par]
            wvs = g % 2
            items = []
            sq_v = {}

            def qk_part(tl, ch, last_of_tile, first_s=None):
                def f():
                    w("tensor", "u%d" % tl, g_qk[(g, tl)])
                    wy("tensor", v_yT, half=ch // 2)
                    bb = next_bank()
                    if first_s is not None:
                        w("tensor", "s", first_s)
                    for k in range(KT):
                        go("tensor", lambda e, _tl=tl, _ch=ch, _k=k, _bb=bb: e.matmul(
                            PS[:, _bb:_bb + 512],
                            RING[:, R_GU + _tl * 1024 + _k * 128: R_GU + _tl * 1024 + _k * 128 + 128],
                            yT[:, _k * S + _ch * 512: _k * S + _ch * 512 + 512],
                            start=(_k == 0), stop=(_k == KT - 1)))
                    t_m = mark("tensor", "t")
                    if last_of_tile:
                        st["qk_slot_t"][tl] = t_m
                    et = (2 * g + tl % 2) if tl < 2 else (8 + 2 * g + tl % 2)
                    w("vector", "t", t_m)
                    w("vector", "f0", d_bqk)
                    v_e = go("vector", lambda e, _tl=tl, _ch=ch, _et=et, _bb=bb: e.tensor_scalar(
                        ARN[:, qkb + _tl * 2048 + _ch * 512: qkb + _tl * 2048 + _ch * 512 + 512],
                        PS[:, _bb:_bb + 512],
                        bqk[:, _et:_et + 1], None, ALU.add), sem="v")
                    done_bank(bb, v_e)
                    st["qk_done_v"][g] = v_e
                return f

            def sq_item(tl):
                def f():
                    slot = tl % 2
                    w("vector", "t", st["sqn_free_t"][slot])
                    w("vector", "v", st["qk_done_v"][g])
                    sq_v[tl] = go("vector", lambda e, _tl=tl, _sl=slot: e.tensor_mul(
                        ARN[:, A_SQN + _sl * 2048: A_SQN + _sl * 2048 + 2048],
                        ARN[:, qkb + _tl * 2048: qkb + _tl * 2048 + 2048],
                        ARN[:, qkb + _tl * 2048: qkb + _tl * 2048 + 2048]), sem="v")
                return f

            def norm_item(tl, ch, idx):
                def f():
                    slot = tl % 2
                    w("tensor", "v", sq_v[tl])
                    bb = next_bank()
                    go("tensor", lambda e, _sl=slot, _ch=ch, _bb=bb: e.matmul(
                        PS[0:1, _bb:_bb + 512], ones[:, 0:1],
                        ARN[:, A_SQN + _sl * 2048 + _ch * 512: A_SQN + _sl * 2048 + _ch * 512 + 512],
                        start=True, stop=True))
                    t_m = mark("tensor", "t")
                    st["sqn_free_t"][slot] = t_m
                    w("vector", "t", t_m)
                    v_r = go("vector", lambda e, _i=idx, _bb=bb: e.reduce_max(
                        nmx[0:1, _i:_i + 1], PS[0:1, _bb:_bb + 512], axis=AX.X), sem="v")
                    done_bank(bb, v_r)
                return f

            def stab_item():
                def f():
                    go("vector", lambda e: e.reduce_max(
                        nmx[0:1, 0:1], nmx[0:1, 0:16], axis=AX.X))
                    v_nc = go("vector", lambda e, _g=g: e.tensor_scalar(
                        nb16[0:1, _g:_g + 1], nmx[0:1, 0:1], -0.125, None, ALU.mult),
                        sem="v")
                    w("tensor", "v", v_nc)
                    bb = next_bank()
                    go("tensor", lambda e, _g=g, _bb=bb: e.matmul(
                        PS[0:128, _bb:_bb + 1], ones[0:1, 0:128], nb16[0:1, _g:_g + 1],
                        start=True, stop=True))
                    t_m = mark("tensor", "t")
                    w("vector", "t", t_m)
                    v_n = go("vector", lambda e, _g=g, _bb=bb: e.tensor_copy(
                        negc[:, _g:_g + 1], PS[:, _bb:_bb + 1]), sem="v")
                    done_bank(bb, v_n)
                    st["negc_v"][g] = v_n
                return f

            def memset_item():
                def f():
                    w("vector", "t", st["vext_free_t"][par])
                    go("vector", lambda e: e.memset(
                        ARN[:, vxb:vxb + 4160], 1.0), sem="v")
                return f

            def v_part(stl, last):
                def f():
                    w("tensor", "n%d" % wvs, g_wv[g])
                    wy("tensor", v_yT, half=stl // 8)
                    bb = next_bank()
                    for k in range(KT):
                        go("tensor", lambda e, _k=k, _stl=stl, _bb=bb: e.matmul(
                            PS[:, _bb:_bb + 256],
                            yT[:, _k * S + _stl * 128: _k * S + _stl * 128 + 128],
                            RING[:, R_DN + wvs * 2048 + _k * 256: R_DN + wvs * 2048 + _k * 256 + 256],
                            start=(_k == 0), stop=False))
                    w("tensor", "gv", d_bv)
                    w("tensor", "v", v_ones)
                    go("tensor", lambda e, _stl=stl, _g=g, _bb=bb: e.matmul(
                        PS[:, _bb:_bb + 256],
                        ones[0:1, 0:128], bv16[0:1, _g * 256:_g * 256 + 256],
                        start=False, stop=True))
                    t_m = mark("tensor", "t")
                    if last:
                        st["wv_slot_t"][wvs] = t_m
                    w("vector", "t", t_m)
                    v_e = go("vector", lambda e, _stl=stl, _bb=bb: e.tensor_copy(
                        ARN[:, vxb:vxb + 4160]
                        .rearrange("p (h j c) -> p h j c", h=4, j=16)[:, :, _stl, 0:64],
                        PS[:, _bb:_bb + 256].rearrange("p (h c) -> p h c", h=4)), sem="v")
                    done_bank(bb, v_e)
                    st["vext_done_v"][g] = v_e
                return f

            first_s = st.pop("g_first_s", None)
            for tl in range(4):
                for ch in range(4):
                    fs = first_s if (tl == 0 and ch == 0) else None
                    items.append(qk_part(tl, ch, ch == 3, fs))
                items.append(sq_item(tl))
                for ch in range(4):
                    items.append(norm_item(tl, ch, tl * 4 + ch))
            items.append(stab_item())
            items.append(memset_item())
            for stl in range(16):
                items.append(v_part(stl, stl == 15))
            return items

        def emit_fcin_load(sc4, ks):
            def f():
                ic = sc4 // 2
                for k in ks:
                    slot = st["fcin_seq"] % 12
                    st["fcin_seq"] += 1
                    w("sync", "o%d" % ic, st["d_ot_map"][(2 * k + 1, ic)])
                    w("sync", "t", st["fcin_slot_t"][slot])
                    d_in = dma("sync", "f%d" % (slot % 4),
                               ARN[:, slot * 512: slot * 512 + 512],
                               OT[k][:, sc4 * 512: sc4 * 512 + 512])
                    st["fcin_d"][(k, sc4)] = (d_in, slot)
            return f

        def emit_fc_chunk(sc4, et, ks, final, v_yT=None):
            def f():
                bb = next_bank()
                w("tensor", "fcw", st["g_fc"][et])
                for i, k in enumerate(ks):
                    d_in, slot = st["fcin_d"][(k, sc4)]
                    w("tensor", "f%d" % (slot % 4), d_in)
                    go("tensor", lambda e, _et=et, _k=k, _sl=slot, _bb=bb,
                       _st=(i == 0), _sp=(i == len(ks) - 1): e.matmul(
                        PS[:, _bb:_bb + 512],
                        RING[:, _et * 1024 + _k * 128: _et * 1024 + _k * 128 + 128],
                        ARN[:, _sl * 512: _sl * 512 + 512],
                        start=_st, stop=_sp))
                t_m = mark("tensor", "t")
                st["fc_t_last"] = t_m
                if et == 7:
                    for k in ks:
                        st["fcin_slot_t"][st["fcin_d"][(k, sc4)][1]] = t_m
                w("vector", "t", t_m)
                if not final:
                    v_e = go("vector", lambda e, _et=et, _sc4=sc4, _bb=bb:
                             e.scalar_tensor_tensor(
                                 rT[:, _et * S + _sc4 * 512: _et * S + _sc4 * 512 + 512],
                                 PS[:, _bb:_bb + 512], 1.0,
                                 rT[:, _et * S + _sc4 * 512: _et * S + _sc4 * 512 + 512],
                                 ALU.mult, ALU.add), sem="v")
                else:
                    w("vector", "f0", d_bfc)
                    if v_yT and v_yT.get("gv"):
                        w("vector", "gv", v_yT["gv"])
                    sl2 = et % 2
                    v_e = go("vector", lambda e, _et=et, _sl=sl2, _sc4=sc4, _bb=bb:
                             e.scalar_tensor_tensor(
                                 tfb[:, _sl * 512: _sl * 512 + 512],
                                 PS[:, _bb:_bb + 512], bfc[:, _et:_et + 1],
                                 yT[:, _et * S + _sc4 * 512: _et * S + _sc4 * 512 + 512],
                                 ALU.add, ALU.add), sem="v")
                    v_e = go("vector", lambda e, _et=et, _sl=sl2, _sc4=sc4: e.tensor_add(
                        rT[:, _et * S + _sc4 * 512: _et * S + _sc4 * 512 + 512],
                        rT[:, _et * S + _sc4 * 512: _et * S + _sc4 * 512 + 512],
                        tfb[:, _sl * 512: _sl * 512 + 512]), sem="v")
                done_bank(bb, v_e)
                st["fc_v_last"] = v_e
            return f

        def build_fc_fill_items():
            for et in range(8):
                if et < 4:
                    w("gpsimd", "t", st["qk_slot_t"][et])
                w("gpsimd", "t", st["wv_slot_t"][0])
                w("gpsimd", "t", st["wv_slot_t"][1])
                st["g_fc"][et] = dma("gpsimd", "fcw",
                                     RING[:, et * 1024:(et + 1) * 1024], WFC[et])
            items = [emit_fcin_load(0, range(6)), emit_fcin_load(1, range(6))]
            for sc4 in range(4):
                if sc4 + 2 < 4:
                    items.append(emit_fcin_load(sc4 + 2, range(6)))
                for et in range(8):
                    items.append(emit_fc_chunk(sc4, et, list(range(6)), False))
            return items

        def emit_core(g):
            par = g % 2
            qkb = A_QK[par]
            vxb = A_VX[par]
            units = [(hl, ic, j) for hl in range(GH) for ic in range(2)
                     for j in range(16)]
            fill = build_fill_items(g + 1) if g + 1 < NG else build_fc_fill_items()
            fi = [0]
            qk_emitted = [False]
            v_emitted = [False]

            def drip_fill(target_idx):
                want = (target_idx * len(fill)) // len(units) if fill else 0
                while fi[0] < min(want, len(fill)):
                    fill[fi[0]]()
                    fi[0] += 1
                    if fill and fi[0] == 36 and not qk_emitted[0] and g + 2 < NG:
                        emit_loads_qk(g + 2)
                        qk_emitted[0] = True
                    if fill and fi[0] == len(fill) and not v_emitted[0] and g + 2 < NG:
                        emit_loads_wv(g + 2)
                        v_emitted[0] = True

            t_sc = {}

            def emit_scores(idx):
                hl, ic, j = units[idx]
                sbk = idx % 2
                w("tensor", "s", st["sc_bank_s"][sbk])
                w("tensor", "v", st.get("pro_v", 0))
                w("tensor", "v", st["qk_done_v"][g])
                qtile, ktile, roff = hl // 2, 2 + hl // 2, (hl % 2) * 64
                icc = ic * 1024
                for n in range(2):
                    go("tensor", lambda e, _sbk=sbk, _n=n, _qt=qtile, _kt=ktile,
                       _ro=roff, _j=j, _icc=icc: e.matmul(
                        PS[:, _sbk * 1024 + _n * 512: _sbk * 1024 + _n * 512 + 512],
                        ARN[_ro:_ro + 64, qkb + _kt * 2048 + _j * 128: qkb + _kt * 2048 + _j * 128 + 128],
                        ARN[_ro:_ro + 64, qkb + _qt * 2048 + _icc + _n * 512: qkb + _qt * 2048 + _icc + _n * 512 + 512],
                        start=True, stop=True))
                t_sc[idx] = mark("tensor", "t")

            def epilogue_a(pend):
                h, ic, t_av = pend
                v_ev = None
                for n in range(2):
                    w("vector", "t", t_av)
                    w("vector", "v", st["sg_free_v"])
                    v_ev = go("vector", lambda e, _n=n: e.tensor_copy(
                        sgb[0:65, _n * 512: _n * 512 + 512],
                        PS[0:65, 2048 + _n * 512: 2048 + _n * 512 + 512]), sem="v")
                    st["av_free_v"][n] = v_ev
                st["last_ev_v"] = v_ev

            def emit_recip():
                w("scalar", "v", st["last_ev_v"])
                go("scalar", lambda e: e.activation(
                    tfb[64:65, 0:1024], sgb[64:65, 0:1024], AF.Ln))
                s_rv = go("scalar", lambda e: e.activation(
                    rvb[64:65, 0:1024], tfb[64:65, 0:1024], AF.Exp, scale=-1.0),
                    sem="s")
                return s_rv

            def flush_epilogue(pend, s_rv):
                h, ic, t_av = pend
                icc = ic * 1024
                oslot = (h * 2 + ic) % 2
                v_mu = None
                for n in range(2):
                    bb = next_bank()
                    w("tensor", "s", s_rv)
                    go("tensor", lambda e, _n=n, _bb=bb: e.matmul(
                        PS[0:64, _bb:_bb + 512], ones[64:65, 0:64],
                        rvb[64:65, _n * 512:_n * 512 + 512],
                        start=True, stop=True))
                    t_b = mark("tensor", "t")
                    w("vector", "t", t_b)
                    if n == 0:
                        w("vector", "o%d" % oslot, st["obuf_slot_d"][oslot])
                    v_mu = go("vector", lambda e, _os=oslot, _n=n, _bb=bb: e.tensor_mul(
                        obuf[0:64, _os * 1024 + _n * 512: _os * 1024 + _n * 512 + 512],
                        sgb[0:64, _n * 512:_n * 512 + 512],
                        PS[0:64, _bb:_bb + 512]), sem="v")
                    done_bank(bb, v_mu)
                st["sg_free_v"] = v_mu
                w("sync", "v", v_mu)
                d_ot = dma("sync", "o%d" % oslot,
                           OT[h // 2][(h % 2) * 64:(h % 2) * 64 + 64, icc:icc + 1024],
                           obuf[0:64, oslot * 1024: oslot * 1024 + 1024])
                st["obuf_slot_d"][oslot] = d_ot
                st["d_ot_map"][(h, ic)] = d_ot

            emit_scores(0)
            emit_scores(1)
            pending = None
            pend_srv = None
            for idx, (hl, ic, j) in enumerate(units):
                h = g * GH + hl
                sbk = idx % 2
                pslot = idx % 4
                w("scalar", "t", t_sc[idx])
                w("scalar", "t", st["p_slot_t"][pslot])
                w("scalar", "v", st["negc_v"][g])
                s_exp = go("scalar", lambda e, _sbk=sbk, _ps=pslot, _g=g: e.activation(
                    ARN[:, A_P + _ps * 1024: A_P + _ps * 1024 + 1024],
                    PS[:, _sbk * 1024: _sbk * 1024 + 1024],
                    AF.Exp, bias=negc[:, _g:_g + 1], scale=0.125), sem="s")
                st["sc_bank_s"][sbk] = s_exp
                if idx + 2 < len(units):
                    emit_scores(idx + 2)
                if pending is not None:
                    if pend_srv is None and idx - pend_idx >= 2:
                        pend_srv = emit_recip()
                    elif pend_srv is not None and idx - pend_idx >= 3:
                        flush_epilogue(pending, pend_srv)
                        pending = None
                        pend_srv = None
                w("tensor", "s", s_exp)
                if j == 0:
                    w("tensor", "v", st["vext_done_v"][g])
                for n in range(2):
                    if j == 0:
                        w("tensor", "v", st["av_free_v"][n])
                    go("tensor", lambda e, _hl=hl, _j=j, _n=n, _ps=pslot, _vx=vxb: e.matmul(
                        PS[0:65, 2048 + _n * 512: 2048 + _n * 512 + 512],
                        ARN[:, _vx + (_hl * 16 + _j) * 65: _vx + (_hl * 16 + _j) * 65 + 65],
                        ARN[:, A_P + _ps * 1024 + _n * 512: A_P + _ps * 1024 + _n * 512 + 512],
                        start=(_j == 0), stop=(_j == 15)))
                t_av = mark("tensor", "t")
                st["p_slot_t"][pslot] = t_av
                st["last_av_t"] = t_av
                if j == 15:
                    pending = (h, ic, t_av)
                    epilogue_a(pending)
                    pend_srv = None
                    pend_idx = idx
                drip_fill(idx)
            drip_fill(len(units))
            while fi[0] < len(fill):
                fill[fi[0]]()
                fi[0] += 1
            if fill and g + 2 < NG and not v_emitted[0]:
                emit_loads_wv(g + 2)
            if pend_srv is None:
                pend_srv = emit_recip()
            flush_epilogue(pending, pend_srv)
            st["vext_free_t"][par] = st["last_av_t"]

        emit_loads_qk(0)
        emit_loads_wv(0)
        pool_set((0, 512, 1024, 1536, 2048, 2560))
        for it in build_fill_items(0):
            it()
        emit_loads_qk(1)
        emit_loads_wv(1)
        pro_v = CTR["v"]
        st["sc_bank_s"] = [max(x, 0) for x in st["sc_bank_s"]]
        st["av_free_v"] = [max(x, pro_v) for x in st["av_free_v"]]
        st["pro_v"] = pro_v
        pool_set((3072, 3584))
        st["emit_fcin_load"] = emit_fcin_load
        st["emit_fc_chunk"] = emit_fc_chunk
        for g in range(NG):
            emit_core(g)
        return [CTR["o0"], CTR["o1"]]

    def emit_fc(v_yT, guard=None):
        st = attn_state
        for sc4 in range(4):
            st["emit_fcin_load"](sc4, (6, 7))()
        for sc4 in range(4):
            for et in range(8):
                st["emit_fc_chunk"](sc4, et, [6, 7], True, v_yT)()
        t_last = st["fc_t_last"]
        ffn_state["gu_slot_t"] = [t_last] * 4
        ffn_state["dn_slot_t"] = [t_last] * 2
        return st["fc_v_last"]

    v_r = None
    v_y = emit_ln(dep_v=None, dep_d=d_x)
    if stop_phase > 1:
        v_r = emit_ffn(v_y, guard=snap(), **(ffn_kw or {}))
        attn_state["qk_slot_t"] = list(ffn_state["gu_slot_t"])[:4]
        dn_last = max(ffn_state["dn_slot_t"])
        attn_state["wv_slot_t"] = [dn_last, dn_last]
    if stop_phase > 2:
        v_y = emit_ln(dep_v=v_r, guard=snap())
    if stop_phase > 3:
        d_ot_all = emit_attn(v_y, guard=snap())
    if stop_phase > 4:
        v_r = emit_fc(v_y, guard=snap())
    if stop_phase > 5:
        v_y = emit_ln(dep_v=v_r, guard=snap())
    if stop_phase > 6:
        v_r = emit_ffn(v_y, guard=snap())

    if dbg == "yT":
        for k in range(KT):
            w("gpsimd", "v", CTR["v"])
            dma("gpsimd", "gv", OUT[k], yT[:, k * S:(k + 1) * S])
        w("sync", "gv", CTR["gv"])
    else:
        for k in range(KT):
            w("sync", "v", ffn_state["dt_done_v"].get(k, CTR["v"]))
            dma("sync", "do", OUT[k], rT[:, k * S:(k + 1) * S])
    for sname in ("do", "d0", "gv", "u0", "u1", "u2", "u3", "n0",
                  "n1", "fcw", "f0", "f1", "f2", "f3", "o0", "o1"):
        w("sync", sname, CTR[sname])

    with nc.Block() as block:
        block.sync(lambda e: [f(e) for f in OPS["sync"]])
        block.gpsimd(lambda e: [f(e) for f in OPS["gpsimd"]])
        block.vector(lambda e: [f(e) for f in OPS["vector"]])
        block.scalar(lambda e: [f(e) for f in OPS["scalar"]])
        block.tensor(lambda e: [f(e) for f in OPS["tensor"]])
    es.close()
    return nc


def _prep(inputs):
    x = np.asarray(inputs["L0_attn_x"], dtype=np.float32)
    qkv_w = np.asarray(inputs["L0_attn_qkv_w"], dtype=np.float32)
    qkv_b = np.asarray(inputs["L0_attn_qkv_b"], dtype=np.float32)
    fc_w = np.asarray(inputs["L0_attn_fc_w"], dtype=np.float32)
    fc_b = np.asarray(inputs["L0_attn_fc_b"], dtype=np.float32)
    wg = np.asarray(inputs["L0_ffn_w_gate"], dtype=np.int32)
    wu = np.asarray(inputs["L0_ffn_w_up"], dtype=np.int32)
    wd = np.asarray(inputs["L0_ffn_w_down"], dtype=np.int32)

    idx_q = (np.arange(16)[:, None] * 192 + np.arange(64)[None, :]).ravel()
    perm = np.concatenate([idx_q, idx_q + 64, idx_q + 128])
    wT = np.ascontiguousarray(qkv_w[perm].T)
    b_perm = qkv_b[perm]

    wqk = np.ascontiguousarray(
        wT[:, :2048].reshape(KT, 128, 16, 128).transpose(2, 1, 0, 3))
    wv = np.ascontiguousarray(
        wT[:, 2048:].reshape(KT, 128, NG, 256).transpose(2, 1, 0, 3))
    wfc = np.ascontiguousarray(
        fc_w.T.reshape(KT, 128, 8, 128).transpose(2, 1, 0, 3))
    wgt = np.ascontiguousarray(
        wg.T.reshape(KT, 128, HT, 128).transpose(2, 1, 0, 3))
    wut = np.ascontiguousarray(
        wu.T.reshape(KT, 128, HT, 128).transpose(2, 1, 0, 3))
    wdt = np.ascontiguousarray(
        wd.T.reshape(HT, 128, 8, 128).transpose(2, 1, 0, 3))
    bqk = np.ascontiguousarray(b_perm[:2048].reshape(16, 128).T)
    bv = np.ascontiguousarray(b_perm[2048:].reshape(1, 1024))
    bfc = np.ascontiguousarray(fc_b.reshape(8, 128).T)

    shared = {"wqk": wqk, "wv": wv, "wfc": wfc, "wg": wgt, "wu": wut,
              "wd": wdt, "bqk": bqk, "bv": bv, "bfc": bfc}
    in_maps = []
    for b in range(8):
        m = dict(shared)
        m["xt"] = np.ascontiguousarray(x[b].T).reshape(KT, 128, S)
        in_maps.append(m)
    return in_maps


def kernel(**inputs):
    in_maps = _prep(inputs)
    nc = build_nc()
    res = run_bass_kernel_spmd(nc, in_maps, core_ids=list(range(8)))
    out = np.empty((8, S, D), dtype=np.float32)
    for b in range(8):
        ot = np.asarray(res.results[b]["out"], dtype=np.float32)
        out[b] = ot.reshape(D, S).T
    return out
